# version 12
# baseline (speedup 1.0000x reference)
import numpy as np
import ml_dtypes
import concourse.bass as bass
import concourse.mybir as mybir
from concourse.bass_utils import run_bass_kernel_spmd

F32 = mybir.dt.float32
BF16 = mybir.dt.bfloat16
AF = mybir.ActivationFunctionType
ALU = mybir.AluOpType
AX = mybir.AxisListType.X
AP = bass.AP

S, D, NH, HD, NROT = 2048, 2048, 32, 64, 16
HID, HC = 5461, 43
HIDP = HC * 128
TS, DC, NG, GH, NB, TQB = 16, 16, 8, 4, 8, 256
FFN_SUP = [3] * 14 + [1]
HSW = 3 * 128

TAPS = ()


def build(taps=(), stage=99, ngroups=8, do_quant=True):
    nc = bass.Bass("TRN2", target_bir_lowering=False, debug=False)
    x_d = nc.dram_tensor("x", [S, D], F32, kind="ExternalInput").ap()
    wqkv_d = nc.dram_tensor("wqkv", [D, 3 * D], F32, kind="ExternalInput").ap()
    bqkv_d = nc.dram_tensor("bqkv", [1, 3 * D], F32, kind="ExternalInput").ap()
    wout_d = nc.dram_tensor("wout", [D, D], F32, kind="ExternalInput").ap()
    bout_d = nc.dram_tensor("bout", [1, D], F32, kind="ExternalInput").ap()
    w1_d = nc.dram_tensor("w1", [HID, D], F32, kind="ExternalInput").ap()
    w3_d = nc.dram_tensor("w3", [HID, D], F32, kind="ExternalInput").ap()
    w2_d = nc.dram_tensor("w2", [D, HID], F32, kind="ExternalInput").ap()
    cosd_d = nc.dram_tensor("cosdup", [S, 32], F32, kind="ExternalInput").ap()
    sin_d = nc.dram_tensor("sintab", [S, NROT], F32, kind="ExternalInput").ap()
    nsin_d = nc.dram_tensor("nsintab", [S, NROT], F32, kind="ExternalInput").ap()
    id_d = nc.dram_tensor("ident", [128, 128], F32, kind="ExternalInput").ap()
    m0_d = nc.dram_tensor("mask0", [128, TQB], BF16, kind="ExternalInput").ap()
    m1_d = nc.dram_tensor("mask1", [128, TQB], BF16, kind="ExternalInput").ap()
    out_d = nc.dram_tensor("out", [S, D], F32, kind="ExternalOutput").ap()
    w1q_d = nc.dram_tensor("w1q", [HIDP, D], BF16, kind="Internal").ap()
    w3q_d = nc.dram_tensor("w3q", [HIDP, D], BF16, kind="Internal").ap()
    w2q_d = nc.dram_tensor("w2q", [D, HIDP], BF16, kind="Internal").ap()
    tap_d = {}
    for t in taps:
        shp, dt = {"qT": ([128, 2 * S], BF16), "kT": ([128, 2 * S], BF16),
                   "vaug": ([128, TS * 260], BF16), "oUT": ([128, DC * S], BF16),
                   "acc1": ([128, DC * S], F32), "x1": ([128, DC * S], F32),
                   "w1q": ([HIDP, D], BF16), "w2q": ([D, HIDP], BF16),
                   "pcol": ([128, 104], F32),
                   "scales": ([1, 16], F32)}[t]
        tap_d[t] = nc.dram_tensor("tap_" + t, shp, dt, kind="ExternalOutput").ap()

    from contextlib import ExitStack
    es = ExitStack()

    def sbL(name, shape, dt):
        g = nc.sbuf_tensor(name, shape, dt, side="left")
        return g, g.__enter__()

    def sbR(name, shape, dt):
        g = nc.sbuf_tensor(name, shape, dt, side="right")
        return g, g.__enter__()

    pg = nc.psum_tensor("PB", [128, 4096], F32)
    PB = pg.__enter__()

    def pb(bank, cols=512, rows=128, off=0):
        c0 = bank * 512 + off
        return PB[0:rows, c0:c0 + cols]

    g_id, c_id = sbR("c_id", [128, 128], F32)
    g_of, c_onesf = sbR("c_onesf", [128, 128], F32)
    g_ob, c_onesb = sbR("c_onesb", [128, 128], BF16)
    g_n8, c_neg8 = sbR("c_neg8", [128, 1], F32)
    g_bo, c_bo = sbR("c_bo", [128, DC], F32)
    g_sc, s_sc = sbR("s_sc", [128, 8], F32)
    g_sr, s_row = sbR("s_row", [1, 16], F32)
    g_oUT, oUT = sbR("oUT", [128, DC * S], BF16)
    g_cd, c_cosd = sbR("c_cosd", [128, TS * 32], F32)
    g_si, c_sin = sbR("c_sin", [128, TS * NROT], F32)
    g_ns, c_nsin = sbR("c_nsin", [128, TS * NROT], F32)
    g_m0, c_m0 = sbR("c_m0", [128, TQB], BF16)
    g_m1, c_m1 = sbR("c_m1", [128, TQB], BF16)
    g_bq, c_bqg = sbR("c_bqg", [1, 3 * 256], BF16)
    g_pr, probsT = sbR("probsT", [128, 2 * 16 * TQB], BF16)
    g_ar, arep = sbR("arep", [128, TQB], F32)
    g_xT, xT = sbL("xT", [128, DC * 2048], BF16)
    gw0, ws0 = sbL("wslot0", [128, 8 * 256], BF16)
    gw1, ws1 = sbL("wslot1", [128, 8 * 256], BF16)
    gw2, ws2 = sbL("wslot2", [128, 8 * 256], BF16)
    wslot = [ws0, ws1, ws2]
    g_qf, qf32r = sbL("qf32r", [128, 2 * 256], F32)
    g_rt, ropetmp = sbL("ropetmp", [128, 2 * 128], F32)
    g_qT, qT = sbL("qT", [128, 2 * 2048], BF16)
    g_kT, kT = sbL("kT", [128, 2 * 2048], BF16)
    g_va, vaug = sbL("vaug", [128, TS * 260], BF16)
    g_wr, wraw = sbL("wraw", [128, 2 * 1024], F32)
    g_bt, btile = sbL("btile", [128, 1024], BF16)
    g_tn, tern = sbL("tern", [128, 2 * 1024], BF16)
    g_pc, pcol = sbL("pcol", [128, 104], F32)

    class Sem:
        def __init__(self, name):
            self.h = es.enter_context(nc.semaphore(name))
            self.n = 0

        def bump(self, k=1):
            self.n += k
            return self.n

    names = ("sLD sXL sXT sXE sWQ sBQ sQP sRP sTR sTE sVE sST sEX sMS sAV sRC sRE sRS sOE "
             "sWR sMR sSC sTN sTW sWO sXQ sAI sOP sA1 sCS sSM sMV sRL sNM "
             "sFW sX1 sH1 sSI sH3 sGG sYP sA2 sOV sDO sTP "
             "dXL0 dXL1 dWR0 dWR1 dWQ0 dWQ1 dWQ2 dWO0 dWO1 dXQ0 dXQ1 dTW0 dTW1 "
             "dF1 dF3 dFF2 dDO0 dDO1").split()
    SM = {n: Sem(n) for n in names}
    (sLD, sXL, sXT, sXE, sWQ, sBQ, sQP, sRP, sTR, sTE, sVE, sST, sEX, sMS, sAV, sRC, sRE,
     sRS, sOE, sWR, sMR, sSC, sTN, sTW, sWO, sXQ, sAI, sOP, sA1, sCS, sSM, sMV, sRL, sNM,
     sFW, sX1, sH1, sSI, sH3, sGG, sYP, sA2, sOV, sDO, sTP,
     dXL0, dXL1, dWR0, dWR1, dWQ0, dWQ1, dWQ2, dWO0, dWO1, dXQ0, dXQ1, dTW0, dTW1,
     dF1, dF3, dFF2, dDO0, dDO1) = (SM[n] for n in names)
    SXL = [dXL0, dXL1]
    SWR = [dWR0, dWR1]
    SWQ = [dWQ0, dWQ1, dWQ2]
    SWO = [dWO0, dWO1]
    SXQ = [dXQ0, dXQ1]
    STW = [dTW0, dTW1]
    SDO = [dDO0, dDO1]
    SF = [dF1, dF3, dFF2]

    Q = {"t": [], "v": [], "s": [], "g": [], "y": []}

    def W(q, sem, val):
        if val > 0:
            Q[q].append(lambda e, s=sem.h, v=val: e.wait_ge(s, v))

    def run(q, meth, *args, inc=None, iv=1, **kw):
        def f(e, meth=meth, args=args, kw=kw, inc=inc, iv=iv):
            ins = getattr(e, meth)(*args, **kw)
            if inc is not None:
                ins.then_inc(inc.h, iv)
        Q[q].append(f)

    def _finish():
        with nc.Block() as block:
            @block.tensor
            def _(e):
                for f in Q["t"]:
                    f(e)

            @block.vector
            def _(e):
                for f in Q["v"]:
                    f(e)

            @block.scalar
            def _(e):
                for f in Q["s"]:
                    f(e)

            @block.gpsimd
            def _(e):
                for f in Q["g"]:
                    f(e)

            @block.sync
            def _(e):
                for f in Q["y"]:
                    f(e)
        es.close()
        return nc

    for dst, src in ((c_id[:, :], id_d), (c_m0[:, :], m0_d), (c_m1[:, :], m1_d)):
        run("y", "dma_start", out=dst, in_=src, inc=sLD, iv=16)
        sLD.bump(16)
    run("y", "dma_start", out=c_cosd[:, :].rearrange("p (t j) -> p t j", t=TS),
        in_=cosd_d.rearrange("(t p) j -> p t j", p=128), inc=sLD, iv=16)
    sLD.bump(16)
    run("y", "dma_start", out=c_sin[:, :].rearrange("p (t j) -> p t j", t=TS),
        in_=sin_d.rearrange("(t p) j -> p t j", p=128), inc=sLD, iv=16)
    sLD.bump(16)
    run("y", "dma_start", out=c_nsin[:, :].rearrange("p (t j) -> p t j", t=TS),
        in_=nsin_d.rearrange("(t p) j -> p t j", p=128), inc=sLD, iv=16)
    sLD.bump(16)
    run("y", "dma_start", out=c_bo[:, :], in_=bout_d.rearrange("o (c p) -> (o p) c", p=128),
        allow_slow_non_contiguous=True, inc=sLD, iv=16)
    sLD.bump(16)
    NCONST = sLD.n
    run("g", "memset", c_onesf[:, :], 1.0)
    run("g", "memset", c_onesb[:, :], 1.0)
    run("g", "memset", c_neg8[:, :], -8.0)
    run("g", "memset", pcol[:, :], 0.0)
    run("g", "memset", AP(vaug, 64, [[TS * 260, 128], [260, TS], [65, GH]]), 1.0, inc=sVE)
    sVE.bump()
    MEMS_DONE = sVE.n

    for i in range(32):
        ts, hf = divmod(i, 2)
        slot = i % 2
        W("y", sXT, max(0, (i - 1) * 8))
        run("y", "dma_start", out=wraw[:, slot * 1024:(slot + 1) * 1024],
            in_=x_d[ts * 128:(ts + 1) * 128, hf * 1024:(hf + 1) * 1024],
            inc=SXL[slot], iv=16)
        SXL[slot].bump(16)
        W("t", SXL[slot], SXL[slot].n)
        if i == 0:
            W("t", sLD, NCONST)
        for j in range(8):
            dc = hf * 8 + j
            W("t", sXE, max(0, sXT.n - 1))
            run("t", "transpose", pb(2 + (sXT.n % 2), cols=128),
                wraw[:, slot * 1024 + j * 128: slot * 1024 + (j + 1) * 128], c_id[:, :], inc=sXT)
            bankpar = (sXT.n % 2)
            sXT.bump()
            W("v", sXT, sXT.n)
            run("v", "tensor_copy",
                xT[:, dc * 2048 + ts * 128: dc * 2048 + (ts + 1) * 128],
                pb(2 + ((sXT.n - 1) % 2), cols=128), inc=sXE)
            sXE.bump()
    NXE = sXE.n
    if stage <= 1:
        W("y", sXE, NXE)
        return _finish()

    QQL = []

    def Wq(q, sem, val):
        if val > 0:
            QQL.append((q, lambda e, s=sem.h, v=val: e.wait_ge(s, v)))

    def runq(q, meth, *args, inc=None, iv=1, **kw):
        def f(e, meth=meth, args=args, kw=kw, inc=inc, iv=iv):
            ins = getattr(e, meth)(*args, **kw)
            if inc is not None:
                ins.then_inc(inc.h, iv)
        QQL.append((q, f))

    def pieces_of(mi):
        wd = (w1_d, w3_d, w2_d)[mi]
        wq = (w1q_d, w3q_d, w2q_d)[mi]
        ps = []
        if mi < 2:
            for ch in range(HC):
                rows = 128 if ch < HC - 1 else HID - 128 * (HC - 1)
                for hf in range(2):
                    ps.append((wd[ch * 128: ch * 128 + rows, hf * 1024:(hf + 1) * 1024],
                               wq[ch * 128:(ch + 1) * 128, hf * 1024:(hf + 1) * 1024],
                               rows, 1024, 1024))
        else:
            for ch in range(DC):
                for pi in range(6):
                    fd = 1024 if pi < 5 else HID - 5 * 1024
                    fdw = 1024 if pi < 5 else HIDP - 5 * 1024
                    ps.append((wd[ch * 128:(ch + 1) * 128, pi * 1024: pi * 1024 + fd],
                               wq[ch * 128:(ch + 1) * 128, pi * 1024: pi * 1024 + fdw],
                               128, fd, fdw))
        return ps

    def mean_pieces_of(mi):
        wd = (w1_d, w3_d, w2_d)[mi]
        ps = []
        if mi < 2:
            for ch in range(HC - 1):
                for hf in range(2):
                    ps.append((wd[ch * 128: (ch + 1) * 128, hf * 1024:(hf + 1) * 1024], 1024))
            for half in range(2):
                ps.append((AP(wd.tensor, (HC - 1) * 128 * D + half * 680,
                              [[1360, 128], [1, 680]]), 680))
        else:
            for ch in range(DC):
                for pi in range(6):
                    fd = 1024 if pi < 5 else HID - 5 * 1024
                    ps.append((wd[ch * 128:(ch + 1) * 128, pi * 1024: pi * 1024 + fd], fd))
        return ps

    Wq("y", sXT, NXE)
    pco = 0
    red_at = []
    for mi in range(3 if do_quant else 0):
        ps = mean_pieces_of(mi)
        p0 = pco
        if mi > 0 and "pcol" in tap_d:
            Wq("v", sTP, 16)
        runq("v", "memset", pcol[:, 0:96], 0.0)
        for i, (src, fd) in enumerate(ps):
            slot = pco % 2
            if pco >= 2:
                Wq("y", sMR, red_at[pco - 2])
            runq("y", "dma_start", out=wraw[0:128, slot * 1024: slot * 1024 + fd],
                 in_=src, inc=SWR[slot], iv=16)
            SWR[slot].bump(16)
            Wq("v", SWR[slot], SWR[slot].n)
            runq("v", "tensor_reduce", pcol[0:128, i: i + 1],
                 wraw[0:128, slot * 1024: slot * 1024 + fd],
                 axis=AX, op=ALU.add, apply_absolute_value=True, inc=sMR)
            red_at.append(sMR.bump())
            pco += 1
        npc = pco - p0
        Wq("v", sMR, sMR.n)
        runq("v", "tensor_reduce", pcol[:, 96 + mi: 97 + mi],
             pcol[:, 0:npc], axis=AX, op=ALU.add, inc=sMR)
        sMR.bump()
        if mi == 0 and "pcol" in tap_d:
            Wq("y", sMR, sMR.n)
            runq("y", "dma_start", out=tap_d["pcol"], in_=pcol[:, :], inc=sTP, iv=16)
            sTP.bump(16)
        Wq("t", sMR, sMR.n)
        Wq("t", sSC, sSC.n)
        runq("t", "matmul", pb(7, cols=1, rows=1), c_onesf[:, 0:1],
             pcol[:, 96 + mi: 97 + mi], start=True, stop=True, inc=sSC)
        sSC.bump()
        cnt = float(HID * D)
        Wq("v", sSC, sSC.n)
        runq("v", "tensor_scalar", s_row[0:1, 2 * mi: 2 * mi + 1], pb(7, cols=1, rows=1),
             0.5 / cnt, None, op0=ALU.mult)
        runq("v", "tensor_scalar", s_row[0:1, 2 * mi + 1: 2 * mi + 2], pb(7, cols=1, rows=1),
             -0.5 / cnt, None, op0=ALU.mult)
        runq("v", "tensor_scalar", s_row[0:1, 6 + mi: 7 + mi], pb(7, cols=1, rows=1),
             1.0 / cnt, None, op0=ALU.mult, inc=sSC)
        sSC.bump()
    if do_quant:
        Wq("v", sSC, sSC.n)
        runq("v", "tensor_tensor", s_row[0:1, 7:8], s_row[0:1, 7:8], s_row[0:1, 8:9],
             op=ALU.mult, inc=sSC)
        sSC.bump()
        Wq("t", sSC, sSC.n)
        for col in range(8):
            runq("t", "matmul", pb(7, cols=1, off=8 + col), c_onesf[0:1, 0:128],
                 s_row[0:1, col:col + 1], start=True, stop=True, inc=sSC)
            sSC.bump()
        Wq("v", sSC, sSC.n)
        for col in range(8):
            runq("v", "tensor_copy", s_sc[:, col:col + 1], pb(7, cols=1, off=8 + col),
                 inc=sSC if col == 7 else None)
        sSC.bump()
    SCALES_READY = sSC.n
    if "scales" in tap_d:
        Wq("y", sSC, SCALES_READY)
        runq("y", "dma_start", out=tap_d["scales"], in_=s_row[0:1, 0:16], inc=sTP, iv=16)
        sTP.bump(16)

    for mi in range(3 if do_quant else 0):
        ps = pieces_of(mi)
        for i, (src, dst, rows, fd, fdw) in enumerate(ps):
            slot = sTN.n % 2
            if sTN.n < 2 and red_at:
                Wq("y", sMR, red_at[-1])
            Wq("y", sTN, max(0, sTN.n - 1))
            runq("y", "dma_start", out=wraw[0:rows, slot * 1024: slot * 1024 + fd],
                 in_=src, inc=SWR[slot], iv=16)
            SWR[slot].bump(16)
            Wq("v", SWR[slot], SWR[slot].n)
            if sTN.n == 0:
                Wq("v", sSC, SCALES_READY)
            Wq("v", STW[slot], STW[slot].n)
            runq("v", "tensor_scalar", btile[0:rows, 0:fd],
                 wraw[0:rows, slot * 1024: slot * 1024 + fd],
                 s_sc[0:rows, 2 * mi + 1: 2 * mi + 2], None, op0=ALU.is_lt)
            if rows < 128 or fdw > fd:
                runq("v", "memset", tern[:, slot * 1024: slot * 1024 + fdw], 0.0)
            runq("v", "scalar_tensor_tensor", tern[0:rows, slot * 1024: slot * 1024 + fd],
                 wraw[0:rows, slot * 1024: slot * 1024 + fd], s_sc[0:rows, 2 * mi: 2 * mi + 1],
                 btile[0:rows, 0:fd], op0=ALU.is_gt, op1=ALU.subtract, inc=sTN)
            sTN.bump()
            Wq("y", sTN, sTN.n)
            runq("y", "dma_start", out=dst, in_=tern[0:128, slot * 1024: slot * 1024 + fdw],
                 inc=STW[slot], iv=16)
            STW[slot].bump(16)
            sTW.bump(16)
    QUANT_DONE = sTW.n

    spl = [0]

    def splice(frac):
        tgt = int(round(len(QQL) * frac))
        while spl[0] < tgt:
            q, f = QQL[spl[0]]
            Q[q].append(f)
            spl[0] += 1

    exp_done_at = {}
    av_done_at = {}
    oe_done_at = {}
    bank_last_exp = [0, 0]
    W("t", sVE, MEMS_DONE)
    W("t", sXE, NXE)
    wq_done = {}
    for g in range(min(NG, ngroups)):
        splice((g + 1) / 10.0)
        W("t", sEX, sEX.n)
        if g == 0:
            W("v", sLD, NCONST)
        for p in range(3):
            pidx = g * 3 + p
            for hh in range(2):
                ppi = pidx * 2 + hh
                sl = ppi % 3
                if ppi >= 3:
                    W("g", sQP, ((ppi - 3) // 2 + 1) * TS)
                c0 = p * D + g * 256
                run("g", "dma_start",
                    out=wslot[sl][:, :].rearrange("p (dc n) -> p dc n", dc=8),
                    in_=wqkv_d[hh * 1024:(hh + 1) * 1024, c0:c0 + 256]
                    .rearrange("(dc p) n -> p dc n", p=128), inc=SWQ[sl], iv=16)
                SWQ[sl].bump(16)
                wq_done[ppi] = (sl, SWQ[sl].n)
            if p == 0:
                W("g", sQP, 3 * g * TS)
                run("g", "dma_start",
                    out=c_bqg[0:1, :].rearrange("o (three n) -> o three n", three=3),
                    in_=AP(bqkv_d.tensor, g * 256, [[1, 1], [2048, 3], [1, 256]]),
                    inc=sBQ, iv=16)
                sBQ.bump(16)
            for ts in range(TS):
                par = ts % 2
                if p < 2:
                    W("t", sRP, max(0, sRP.n - 1))
                else:
                    W("t", sVE, max(0, sVE.n - 1))
                if ts == 0:
                    for ppi_w in (pidx * 2, pidx * 2 + 1):
                        slw, cnt = wq_done[ppi_w]
                        W("t", SWQ[slw], cnt)
                    W("t", sBQ, (g + 1) * 16)
                for dc in range(DC):
                    sl = (pidx * 2 + dc // 8) % 3
                    run("t", "matmul", pb(par, cols=256),
                        xT[:, dc * 2048 + ts * 128: dc * 2048 + (ts + 1) * 128],
                        wslot[sl][:, (dc % 8) * 256:(dc % 8 + 1) * 256],
                        start=(dc == 0), stop=False)
                run("t", "matmul", pb(par, cols=256), c_onesb[0:1, 0:128],
                    c_bqg[0:1, p * 256:(p + 1) * 256], start=False, stop=True, inc=sQP)
                sQP.bump()
                if p < 2:
                    qsl = ts % 2
                    W("v", sQP, sQP.n)
                    if ts >= 2:
                        W("v", sTR, max(0, sTR.n - 2))
                    bank0 = par * 512
                    run("v", "tensor_copy",
                        AP(qf32r, qsl * 256 + 32, [[512, 128], [64, 4], [1, 32]]),
                        AP(PB, bank0 + 32, [[4096, 128], [64, 4], [1, 32]]))
                    run("v", "tensor_tensor",
                        AP(qf32r, qsl * 256, [[512, 128], [64, 4], [1, 32]]),
                        AP(PB, bank0, [[4096, 128], [64, 4], [1, 32]]),
                        AP(c_cosd, ts * 32, [[TS * 32, 128], [0, 4], [1, 32]]), op=ALU.mult)
                    run("v", "tensor_tensor",
                        AP(ropetmp, qsl * 128, [[256, 128], [32, 4], [1, 16]]),
                        AP(PB, bank0 + 16, [[4096, 128], [64, 4], [1, 16]]),
                        AP(c_nsin, ts * 16, [[TS * 16, 128], [0, 4], [1, 16]]), op=ALU.mult)
                    run("v", "tensor_tensor",
                        AP(ropetmp, qsl * 128 + 16, [[256, 128], [32, 4], [1, 16]]),
                        AP(PB, bank0, [[4096, 128], [64, 4], [1, 16]]),
                        AP(c_sin, ts * 16, [[TS * 16, 128], [0, 4], [1, 16]]), op=ALU.mult)
                    run("v", "tensor_tensor",
                        AP(qf32r, qsl * 256, [[512, 128], [64, 4], [1, 32]]),
                        AP(qf32r, qsl * 256, [[512, 128], [64, 4], [1, 32]]),
                        AP(ropetmp, qsl * 128, [[256, 128], [32, 4], [1, 32]]),
                        op=ALU.add, inc=sRP)
                    sRP.bump()
                    dstT = qT if p == 0 else kT
                    W("t", sRP, sRP.n)
                    for r in range(2):
                        W("t", sTE, max(0, sTR.n - 1))
                        run("t", "transpose", pb(2 + (sTR.n % 2), cols=128),
                            qf32r[:, qsl * 256 + r * 128: qsl * 256 + (r + 1) * 128],
                            c_id[:, :], inc=sTR)
                        bpar = (sTR.n) % 2
                        sTR.bump()
                        W("v", sTR, sTR.n)
                        run("v", "tensor_copy",
                            dstT[:, r * 2048 + ts * 128: r * 2048 + (ts + 1) * 128],
                            pb(2 + bpar, cols=128), inc=sTE)
                        sTE.bump()
                else:
                    W("v", sQP, sQP.n)
                    run("v", "tensor_copy",
                        AP(vaug, ts * 260, [[TS * 260, 128], [65, 4], [1, 64]]),
                        AP(PB, par * 512, [[4096, 128], [64, 4], [1, 64]]), inc=sVE)
                    sVE.bump()
        GRP_TE = sTE.n
        for hl in range(GH):
            h = g * GH + hl
            th, rb = hl // 2, 64 * (hl % 2)
            for b in range(NB):
                bi = h * NB + b
                slot = bi % 2
                ncc = 2 * b + 2
                first_ex = sEX.n
                for c in range(ncc):
                    spar = c % 2
                    W("t", sTE, GRP_TE)
                    W("t", sEX, bank_last_exp[spar])
                    if bi >= 2:
                        W("t", sAV, av_done_at[bi - 2])
                    run("t", "matmul", pb(2 + spar, cols=256),
                        kT[rb:rb + 64, th * 2048 + c * 128: th * 2048 + (c + 1) * 128],
                        qT[rb:rb + 64, th * 2048 + b * 256: th * 2048 + (b + 1) * 256],
                        start=True, stop=True, inc=sST)
                    sST.bump()
                    W("s", sST, sST.n)
                    if bi >= 2:
                        W("s", sAV, av_done_at[bi - 2])
                    run("s", "activation", probsT[:, slot * 4096 + c * 256: slot * 4096 + (c + 1) * 256],
                        pb(2 + spar, cols=256), AF.Exp, bias=c_neg8[:, 0:1], scale=0.125,
                        inc=sEX)
                    sEX.bump()
                    bank_last_exp[spar] = sEX.n
                    if c >= ncc - 2:
                        W("v", sEX, sEX.n)
                        run("v", "tensor_tensor",
                            probsT[:, slot * 4096 + c * 256: slot * 4096 + (c + 1) * 256],
                            probsT[:, slot * 4096 + c * 256: slot * 4096 + (c + 1) * 256],
                            c_m0[:, :] if c == ncc - 2 else c_m1[:, :], op=ALU.mult, inc=sMS)
                        sMS.bump()
                W("t", sEX, sEX.n)
                W("t", sMS, sMS.n)
                if bi >= 2:
                    W("t", sOE, oe_done_at[bi - 2])
                for c in range(ncc):
                    run("t", "matmul", pb(4 + bi % 2, cols=256, rows=65),
                        vaug[:, c * 260 + hl * 65: c * 260 + hl * 65 + 65],
                        probsT[:, slot * 4096 + c * 256: slot * 4096 + (c + 1) * 256],
                        start=(c == 0), stop=(c == ncc - 1), inc=(sAV if c == ncc - 1 else None))
                sAV.bump()
                av_done_at[bi] = sAV.n
                W("v", sAV, sAV.n)
                if bi >= 1:
                    W("v", sRE, sRE.n)
                run("v", "reciprocal", arep[64:65, :], pb(4 + bi % 2, cols=256, rows=65)[64:65, :],
                    inc=sRC)
                sRC.bump()
                W("t", sRC, sRC.n)
                if bi >= 1:
                    W("t", sRS, sRS.n)
                run("t", "matmul", pb(6, cols=256, rows=64), c_onesf[64:65, 0:64],
                    arep[64:65, :], start=True, stop=True, inc=sRE)
                sRE.bump()
                W("s", sRE, sRE.n)
                if bi >= 1:
                    W("s", sOE, sOE.n)
                run("s", "copy", arep[0:64, :], pb(6, cols=256, rows=64)[0:64, :], inc=sRS)
                sRS.bump()
                W("v", sRS, sRS.n)
                dcol = (h // 2) * 2048 + b * 256
                run("v", "tensor_tensor", oUT[64 * (h % 2): 64 * (h % 2) + 64, dcol:dcol + 256],
                    pb(4 + bi % 2, cols=256, rows=65)[0:64, :], arep[0:64, :], op=ALU.mult,
                    inc=sOE)
                sOE.bump()
                oe_done_at[bi] = sOE.n
    splice(1.0)
    ATT_OE = sOE.n
    for t, buf in (("qT", qT), ("kT", kT), ("vaug", vaug)):
        if t in tap_d:
            W("y", sOE, ATT_OE)
            run("y", "dma_start", out=tap_d[t], in_=buf[:, :], inc=sTP, iv=16)
            sTP.bump(16)
    if "oUT" in tap_d:
        W("y", sOE, ATT_OE)
        run("y", "dma_start", out=tap_d["oUT"], in_=oUT[:, :], inc=sTP, iv=16)
        sTP.bump(16)
    for t in ("scales", "w1q", "w2q"):
        if t in tap_d and stage <= 2:
            src = {"scales": None, "w1q": w1q_d, "w2q": w2q_d}[t]
            if t == "scales":
                W("y", sSC, SCALES_READY)
                run("y", "dma_start", out=tap_d[t], in_=s_row[0:1, 0:16], inc=sTP, iv=16)
            else:
                W("y", STW[0], STW[0].n)
                W("y", STW[1], STW[1].n)
                run("y", "dma_start", out=tap_d[t], in_=src, inc=sTP, iv=16)
            sTP.bump(16)
    if stage <= 2:
        W("y", sOE, ATT_OE)
        W("y", STW[0], STW[0].n)
        W("y", STW[1], STW[1].n)
        if sTP.n:
            W("y", sTP, sTP.n)
        return _finish()

    for gg in (g_pc, g_tn, g_bt, g_wr, g_va, g_kT, g_qT, g_rt, g_qf, gw2, gw1, gw0, g_xT):
        gg.__exit__(None, None, None)
    g_ar.__exit__(None, None, None)
    g_pr.__exit__(None, None, None)
    g_bq.__exit__(None, None, None)
    g_m1.__exit__(None, None, None)
    g_m0.__exit__(None, None, None)
    g_ns.__exit__(None, None, None)
    g_si.__exit__(None, None, None)
    g_cd.__exit__(None, None, None)

    g_acc, acc = sbL("acc", [128, DC * 2048], F32)
    g_wo, woutr = sbL("woutr", [128, 2 * 2048], BF16)
    g_xq, xqr = sbL("xqr", [128, 2 * 512], F32)

    for i in range(64):
        ts, qd = divmod(i, 4)
        slot = i % 2
        if i == 0:
            W("y", sOE, ATT_OE)
        W("y", sAI, max(0, (i - 1) * 4))
        run("y", "dma_start", out=xqr[:, slot * 512:(slot + 1) * 512],
            in_=x_d[ts * 128:(ts + 1) * 128, qd * 512:(qd + 1) * 512],
            inc=SXQ[slot], iv=16)
        SXQ[slot].bump(16)
        W("t", SXQ[slot], SXQ[slot].n)
        if i == 0:
            W("t", sOE, ATT_OE)
        for j in range(4):
            dc = qd * 4 + j
            W("t", sAI, max(0, (sXT.n - NXE) - 1))
            run("t", "transpose", pb(0 + ((sXT.n - NXE) % 2), cols=128),
                xqr[:, slot * 512 + j * 128: slot * 512 + (j + 1) * 128], c_id[:, :], inc=sXT)
            bpar = (sXT.n - NXE) % 2
            sXT.bump()
            W("s", sXT, sXT.n)
            run("s", "copy", acc[:, dc * 2048 + ts * 128: dc * 2048 + (ts + 1) * 128],
                pb(0 + bpar, cols=128), inc=sAI)
            sAI.bump()
    ACC_INIT = sAI.n

    for c in range(DC):
        slot = c % 2
        if c < 2:
            W("g", sOE, ATT_OE)
        if c >= 2:
            W("g", sOP, c - 1)
        run("g", "dma_start", out=woutr[:, slot * 2048:(slot + 1) * 2048]
            .rearrange("p (dc n) -> p dc n", dc=16),
            in_=wout_d[0:2048, c * 128:(c + 1) * 128].rearrange("(dc p) n -> p dc n", p=128),
            inc=SWO[slot], iv=16)
        SWO[slot].bump(16)
        W("t", SWO[slot], SWO[slot].n)
        if c == 0:
            W("t", sAI, ACC_INIT)
        if c >= 2:
            W("t", sA1, c - 1)
        base = (c % 2) * 2048
        for dc in range(DC):
            for nb2 in range(4):
                run("t", "matmul", PB[0:128, base + nb2 * 512: base + (nb2 + 1) * 512],
                    woutr[:, slot * 2048 + dc * 128: slot * 2048 + (dc + 1) * 128],
                    oUT[:, dc * 2048 + nb2 * 512: dc * 2048 + (nb2 + 1) * 512],
                    start=(dc == 0), stop=(dc == DC - 1),
                    inc=(sOP if (dc == DC - 1 and nb2 == 3) else None))
        sOP.bump()
        W("v", sOP, sOP.n)
        W("v", sLD, NCONST)
        run("v", "scalar_tensor_tensor", acc[:, c * 2048:(c + 1) * 2048],
            PB[0:128, base: base + 2048], c_bo[:, c:c + 1], acc[:, c * 2048:(c + 1) * 2048],
            op0=ALU.add, op1=ALU.add, inc=sA1)
        sA1.bump()
    WOUT_DONE = sA1.n
    if "acc1" in tap_d:
        W("y", sA1, WOUT_DONE)
        run("y", "dma_start", out=tap_d["acc1"], in_=acc[:, :], inc=sTP, iv=16)
        sTP.bump(16)

    g_oUT.__exit__(None, None, None)
    g_xq.__exit__(None, None, None)
    g_wo.__exit__(None, None, None)
    g_cr, castr = sbL("castr", [128, 2 * 2048], BF16)
    g_sq, sqr = sbL("sqr", [128, 2 * 2048], BF16)
    g_mr, mrow = sbL("mrow", [1, 2048], F32)
    g_rs, rstdrow = sbL("rstdrow", [1, 2048], F32)

    def layernorm(colspan_sem_done):
        sm0 = sSM.n
        for dc in range(DC):
            slot = dc % 2
            W("v", colspan_sem_done[0], colspan_sem_done[1])
            if dc >= 2:
                W("v", sSM, sm0 + dc - 1)
            run("v", "tensor_copy", castr[:, slot * 2048:(slot + 1) * 2048],
                acc[:, dc * 2048:(dc + 1) * 2048])
            run("v", "tensor_tensor", sqr[:, slot * 2048:(slot + 1) * 2048],
                castr[:, slot * 2048:(slot + 1) * 2048],
                castr[:, slot * 2048:(slot + 1) * 2048], op=ALU.mult, inc=sCS)
            sCS.bump()
            W("t", sCS, sCS.n)
            if dc == 0:
                W("t", colspan_sem_done[0], colspan_sem_done[1])
            for nb2 in range(4):
                run("t", "matmul", PB[0:1, nb2 * 512:(nb2 + 1) * 512], c_onesb[:, 0:1],
                    castr[:, slot * 2048 + nb2 * 512: slot * 2048 + (nb2 + 1) * 512],
                    start=(dc == 0), stop=(dc == DC - 1))
            for nb2 in range(4):
                run("t", "matmul", PB[0:1, 2048 + nb2 * 512: 2048 + (nb2 + 1) * 512],
                    c_onesb[:, 0:1],
                    sqr[:, slot * 2048 + nb2 * 512: slot * 2048 + (nb2 + 1) * 512],
                    start=(dc == 0), stop=(dc == DC - 1),
                    inc=(sSM if nb2 == 3 else None))
            sSM.bump(1)
        W("v", sSM, sSM.n)
        run("v", "tensor_scalar", mrow[0:1, :], PB[0:1, 0:2048], 1.0 / D, None, op0=ALU.mult)
        run("v", "tensor_tensor", rstdrow[0:1, :], mrow[0:1, :], mrow[0:1, :], op=ALU.mult)
        run("v", "scalar_tensor_tensor", rstdrow[0:1, :], PB[0:1, 2048:4096], 1.0 / D,
            rstdrow[0:1, :], op0=ALU.mult, op1=ALU.subtract)
        run("v", "tensor_scalar", rstdrow[0:1, :], rstdrow[0:1, :], 1e-5, None, op0=ALU.add,
            inc=sMV)
        sMV.bump()
        W("s", sMV, sMV.n)
        run("s", "activation", rstdrow[0:1, :], rstdrow[0:1, :], AF.Sqrt, bias=0.0, scale=1.0,
            inc=sMV)
        sMV.bump()
        W("v", sMV, sMV.n)
        run("v", "reciprocal", rstdrow[0:1, :], rstdrow[0:1, :], inc=sMV)
        sMV.bump()
        W("t", sMV, sMV.n)
        for nb2 in range(4):
            run("t", "matmul", PB[0:128, nb2 * 512:(nb2 + 1) * 512], c_onesf[0:1, 0:128],
                mrow[0:1, nb2 * 512:(nb2 + 1) * 512], start=True, stop=True)
        for nb2 in range(4):
            run("t", "matmul", PB[0:128, 2048 + nb2 * 512: 2048 + (nb2 + 1) * 512],
                c_onesf[0:1, 0:128], rstdrow[0:1, nb2 * 512:(nb2 + 1) * 512],
                start=True, stop=True, inc=(sRL if nb2 == 3 else None))
        sRL.bump()
        for dc in range(DC):
            W("v", sRL, sRL.n)
            run("v", "tensor_tensor", acc[:, dc * 2048:(dc + 1) * 2048],
                acc[:, dc * 2048:(dc + 1) * 2048], PB[0:128, 0:2048], op=ALU.subtract)
            run("v", "tensor_tensor", acc[:, dc * 2048:(dc + 1) * 2048],
                acc[:, dc * 2048:(dc + 1) * 2048], PB[0:128, 2048:4096], op=ALU.mult,
                inc=sNM)
            sNM.bump()

    layernorm((sA1, WOUT_DONE))
    LN1_DONE = sNM.n
    if "x1" in tap_d:
        W("y", sNM, LN1_DONE)
        run("y", "dma_start", out=tap_d["x1"], in_=acc[:, :], inc=sTP, iv=16)
        sTP.bump(16)
    if "w1q" in tap_d:
        W("y", STW[0], STW[0].n)
        W("y", STW[1], STW[1].n)
        run("y", "dma_start", out=tap_d["w1q"], in_=w1q_d, inc=sTP, iv=16)
        sTP.bump(16)
    if "w2q" in tap_d:
        W("y", STW[0], STW[0].n)
        W("y", STW[1], STW[1].n)
        run("y", "dma_start", out=tap_d["w2q"], in_=w2q_d, inc=sTP, iv=16)
        sTP.bump(16)

    if stage <= 3:
        W("y", sNM, LN1_DONE)
        W("y", STW[0], STW[0].n)
        W("y", STW[1], STW[1].n)
        if sTP.n:
            W("y", sTP, sTP.n)
        return _finish()
    for gg in (g_rs, g_mr, g_sq, g_cr):
        gg.__exit__(None, None, None)
    g_x1, x1T = sbL("x1T", [128, DC * 1024], BF16)
    g_w1s, w1s = sbL("w1s", [128, DC * HSW], BF16)
    g_w3s, w3s = sbL("w3s", [128, DC * HSW], BF16)
    g_w2s, w2s = sbL("w2s", [128, 3 * 2048], BF16)
    g_gb, gbuf = sbL("gbuf", [128, 3 * 1024], BF16)

    for hf in range(2):
        for dc in range(DC):
            if hf == 0:
                W("v", sNM, LN1_DONE)
            else:
                W("v", sH3, sH3.n)
                W("v", sH1, sH1.n)
            run("v", "tensor_copy", x1T[:, dc * 1024:(dc + 1) * 1024],
                acc[:, dc * 2048 + hf * 1024: dc * 2048 + hf * 1024 + 1024],
                inc=(sX1 if dc == DC - 1 else None))
        sX1.bump()
        ch0 = 0
        for si, cs in enumerate(FFN_SUP):
            wsup = cs * 128
            if si == 0 and hf == 0:
                if sTW.n:
                    W("y", STW[0], STW[0].n)
                    W("y", STW[1], STW[1].n)
                W("y", sOP, sOP.n)
                W("y", sNM, LN1_DONE)
            W("y", sH1, sH1.n)
            for dc in range(DC):
                run("y", "dma_start", out=w1s[:, dc * HSW: dc * HSW + wsup],
                    in_=w1q_d[ch0 * 128: ch0 * 128 + wsup, dc * 128:(dc + 1) * 128],
                    transpose=True, inc=SF[0], iv=16)
                SF[0].bump(16)
            W("y", sH3, sH3.n)
            for dc in range(DC):
                run("y", "dma_start", out=w3s[:, dc * HSW: dc * HSW + wsup],
                    in_=w3q_d[ch0 * 128: ch0 * 128 + wsup, dc * 128:(dc + 1) * 128],
                    transpose=True, inc=SF[1], iv=16)
                SF[1].bump(16)
            W("y", sYP, sYP.n)
            for ci in range(cs):
                run("y", "dma_start", out=w2s[:, ci * 2048:(ci + 1) * 2048],
                    in_=w2q_d[0:2048, (ch0 + ci) * 128:(ch0 + ci + 1) * 128],
                    transpose=True, inc=SF[2], iv=16)
                SF[2].bump(16)
            W1_T, W3_T, W2_T = SF[0].n, SF[1].n, SF[2].n
            YP_PREV = sYP.n
            for ci in range(cs):
                W("t", SF[0], W1_T)
                W("t", sX1, sX1.n)
                W("t", sSI, max(0, sSI.n))
                for dc in range(DC):
                    for nb2 in range(2):
                        run("t", "matmul", PB[0:128, nb2 * 512:(nb2 + 1) * 512],
                            w1s[:, dc * HSW + ci * 128: dc * HSW + (ci + 1) * 128],
                            x1T[:, dc * 1024 + nb2 * 512: dc * 1024 + (nb2 + 1) * 512],
                            start=(dc == 0), stop=(dc == DC - 1),
                            inc=(sH1 if dc == DC - 1 and nb2 == 1 else None))
                sH1.bump()
                W("s", sH1, sH1.n)
                W("s", sSC, SCALES_READY)
                W("s", sYP, YP_PREV)
                run("s", "activation", gbuf[:, (ci % 3) * 1024:((ci % 3) + 1) * 1024],
                    PB[0:128, 0:1024], AF.Silu, bias=0.0, scale=s_sc[:, 6:7], inc=sSI)
                sSI.bump()
            for ci in range(cs):
                W("t", SF[1], W3_T)
                W("t", sGG, max(0, sGG.n))
                for dc in range(DC):
                    for nb2 in range(2):
                        run("t", "matmul", PB[0:128, 1024 + nb2 * 512: 1024 + (nb2 + 1) * 512],
                            w3s[:, dc * HSW + ci * 128: dc * HSW + (ci + 1) * 128],
                            x1T[:, dc * 1024 + nb2 * 512: dc * 1024 + (nb2 + 1) * 512],
                            start=(dc == 0), stop=(dc == DC - 1),
                            inc=(sH3 if dc == DC - 1 and nb2 == 1 else None))
                sH3.bump()
                W("v", sH3, sH3.n)
                W("v", sSI, sSI.n - (cs - 1 - ci))
                run("v", "scalar_tensor_tensor", gbuf[:, (ci % 3) * 1024:((ci % 3) + 1) * 1024],
                    gbuf[:, (ci % 3) * 1024:((ci % 3) + 1) * 1024], s_sc[:, 7:8],
                    PB[0:128, 1024:2048], op0=ALU.mult, op1=ALU.mult, inc=sGG)
                sGG.bump()
            GG_NOW = sGG.n
            for dc in range(DC):
                ypar = dc % 2
                W("t", sGG, GG_NOW)
                W("t", SF[2], W2_T)
                if sA2.n >= 1:
                    W("t", sA2, max(0, sA2.n - 1))
                for ci in range(cs):
                    for nb2 in range(2):
                        run("t", "matmul",
                            PB[0:128, 2048 + ypar * 1024 + nb2 * 512: 2048 + ypar * 1024 + (nb2 + 1) * 512],
                            w2s[:, ci * 2048 + dc * 128: ci * 2048 + (dc + 1) * 128],
                            gbuf[:, (ci % 3) * 1024 + nb2 * 512: (ci % 3) * 1024 + (nb2 + 1) * 512],
                            start=(ci == 0), stop=(ci == cs - 1),
                            inc=(sYP if ci == cs - 1 and nb2 == 1 else None))
                sYP.bump()
                W("v", sYP, sYP.n)
                run("v", "tensor_tensor",
                    acc[:, dc * 2048 + hf * 1024: dc * 2048 + hf * 1024 + 1024],
                    acc[:, dc * 2048 + hf * 1024: dc * 2048 + hf * 1024 + 1024],
                    PB[0:128, 2048 + ypar * 1024: 2048 + (ypar + 1) * 1024], op=ALU.add,
                    inc=sA2)
                sA2.bump()
            ch0 += cs
    FFN_DONE = sA2.n
    if stage <= 4:
        W("y", sA2, FFN_DONE)
        if sTP.n:
            W("y", sTP, sTP.n)
        return _finish()

    for gg in (g_gb, g_w2s, g_w3s, g_w1s, g_x1):
        gg.__exit__(None, None, None)
    g_cr2, castr = sbL("castr2", [128, 2 * 2048], BF16)
    g_sq2, sqr = sbL("sqr2", [128, 2 * 2048], BF16)
    g_mr2, mrow = sbL("mrow2", [1, 2048], F32)
    g_rs2, rstdrow = sbL("rstdrow2", [1, 2048], F32)
    g_or, outr = sbL("outr", [128, 2 * 2048], F32)

    layernorm((sA2, FFN_DONE))
    LN2_DONE = sNM.n

    for ts in range(TS):
        slot = ts % 2
        for dc in range(DC):
            W("t", sNM, LN2_DONE)
            W("t", sOV, max(0, (sXT.n - NXE - 256) - 1))
            run("t", "transpose", pb(0 + ((sXT.n - NXE - 256) % 2), cols=128),
                acc[:, dc * 2048 + ts * 128: dc * 2048 + (ts + 1) * 128], c_id[:, :],
                inc=sXT)
            bpar = (sXT.n - NXE - 256) % 2
            sXT.bump()
            W("v", sXT, sXT.n)
            if ts >= 2 and dc == 0:
                W("v", SDO[slot], SDO[slot].n)
            run("v", "tensor_copy", outr[:, slot * 2048 + dc * 128: slot * 2048 + (dc + 1) * 128],
                pb(0 + bpar, cols=128), inc=sOV)
            sOV.bump()
        W("y", sOV, sOV.n)
        run("y", "dma_start", out=out_d[ts * 128:(ts + 1) * 128, :],
            in_=outr[:, slot * 2048:(slot + 1) * 2048], inc=SDO[slot], iv=16)
        SDO[slot].bump(16)
    W("y", SDO[0], SDO[0].n)
    W("y", SDO[1], SDO[1].n)
    if sTP.n:
        W("y", sTP, sTP.n)

    return _finish()


def host_constants():
    inv = (10000.0 ** (-(np.arange(NROT, dtype=np.float64)) / NROT))
    t = np.arange(S, dtype=np.float64)
    fr = np.outer(t, inv)
    cos = np.cos(fr).astype(np.float32)
    sin = np.sin(fr).astype(np.float32)
    cosd = np.concatenate([cos, cos], axis=1).astype(np.float32)
    nsin = (-sin).astype(np.float32)
    ident = np.eye(128, dtype=np.float32)
    p = np.arange(128)[:, None]
    n = np.arange(TQB)[None, :]
    m0 = (p <= n).astype(ml_dtypes.bfloat16)
    m1 = (p <= n - 128).astype(ml_dtypes.bfloat16)
    return dict(cosdup=cosd, sintab=sin.astype(np.float32), nsintab=nsin, ident=ident,
                mask0=m0, mask1=m1)


_CACHE = {}


def kernel(L0_attn_x, L0_attn_Wqkv, L0_attn_b_qkv, L0_attn_Wout, L0_attn_b_out,
           L0_ffn_w1, L0_ffn_w3, L0_ffn_w2):
    key = ("nc", tuple(TAPS))
    if key not in _CACHE:
        _CACHE[key] = build(TAPS)
    nc = _CACHE[key]
    consts = host_constants()
    x = np.ascontiguousarray(np.asarray(L0_attn_x, dtype=np.float32))
    base = dict(wqkv=np.ascontiguousarray(np.asarray(L0_attn_Wqkv, np.float32)),
                bqkv=np.ascontiguousarray(np.asarray(L0_attn_b_qkv, np.float32)).reshape(1, -1),
                wout=np.ascontiguousarray(np.asarray(L0_attn_Wout, np.float32)),
                bout=np.ascontiguousarray(np.asarray(L0_attn_b_out, np.float32)).reshape(1, -1),
                w1=np.ascontiguousarray(np.asarray(L0_ffn_w1, np.float32)),
                w3=np.ascontiguousarray(np.asarray(L0_ffn_w3, np.float32)),
                w2=np.ascontiguousarray(np.asarray(L0_ffn_w2, np.float32)),
                **consts)
    in_maps = [dict(x=np.ascontiguousarray(x[b]), **base) for b in range(8)]
    res = run_bass_kernel_spmd(nc, in_maps, core_ids=list(range(8)))
    out = np.stack([res.results[b]["out"] for b in range(8)], axis=0)
    return out
